# version 26
# baseline (speedup 1.0000x reference)
import numpy as np
import concourse.bass as bass
import concourse.mybir as mybir
from concourse.bass_utils import run_bass_kernel_spmd
from concourse.alu_op_type import AluOpType

F32 = mybir.dt.float32
BF16 = mybir.dt.bfloat16
AF = mybir.ActivationFunctionType
AX = mybir.AxisListType

T = 2048
DM = 1024
NT = 16
NF = 8
NJ = 8
NP = 16
EPS = 1e-5
TANH_SCALE = 0.125 / (64.0 * 2.0)


class Ctr:
    def __init__(self, h):
        self.h = h
        self.n = 0

    def bump(self, k=1):
        self.n += k
        return self.n

    def inc(self, instr, k=1):
        instr.then_inc(self.h, k)
        return self.bump(k)


def build_program(stop="full"):
    nc = bass.Bass("TRN2", target_bir_lowering=False)

    x_in = nc.dram_tensor("x_in", [T, DM], F32, kind="ExternalInput").ap()
    w_qkv = nc.dram_tensor("w_qkv", [DM, 3 * DM], F32, kind="ExternalInput").ap()
    w_out = nc.dram_tensor("w_out", [DM, DM], F32, kind="ExternalInput").ap()
    g1r = nc.dram_tensor("g1r", [4, 4, 128, 32], F32, kind="ExternalInput").ap()
    g2r = nc.dram_tensor("g2r", [4, 128, 16], F32, kind="ExternalInput").ap()
    g1i = nc.dram_tensor("g1i", [4, 4, 128, 32], F32, kind="ExternalInput").ap()
    g2i = nc.dram_tensor("g2i", [4, 128, 16], F32, kind="ExternalInput").ap()
    bias_cat = nc.dram_tensor("bias_cat", [1, DM], F32, kind="ExternalInput").ap()
    cmask_d = nc.dram_tensor("cmask", [128, 128], F32, kind="ExternalInput").ap()
    idn_d = nc.dram_tensor("idn", [128, 128], F32, kind="ExternalInput").ap()
    amask_d = nc.dram_tensor("amask", [128, 512], F32, kind="ExternalInput").ap()
    gm25_d = nc.dram_tensor("gm25", [64, T], F32, kind="ExternalInput").ap()
    big2_d = nc.dram_tensor("big2", [128, 160], F32, kind="ExternalInput").ap()
    tlow_d = nc.dram_tensor("tlow", [64, 64], F32, kind="ExternalInput").ap()
    ones2_d = nc.dram_tensor("ones2", [2, 128], F32, kind="ExternalInput").ap()
    onescol_d = nc.dram_tensor("onescol", [128, 1], F32, kind="ExternalInput").ap()
    epscol_d = nc.dram_tensor("epscol", [128, 1], F32, kind="ExternalInput").ap()
    y_out = nc.dram_tensor("y_out", [T, DM], F32, kind="ExternalOutput").ap()
    dbg_a = dbg_b = None
    if stop != "full":
        dbg_a = nc.dram_tensor("dbg_a", [128, NT * DM], BF16, kind="ExternalOutput").ap()
        dbg_b = nc.dram_tensor("dbg_b", [128, NF * T], BF16, kind="ExternalOutput").ap()

    from contextlib import ExitStack
    with ExitStack() as _st:
        def sbt(name, shape, dt):
            return _st.enter_context(nc.sbuf_tensor(name, shape, dt))

        x_tm = sbt("x_tm", [128, NT * DM], F32)
        ln_tm = sbt("ln_tm", [128, NT * DM], BF16)
        ln_fm = sbt("ln_fm", [128, NF * T], BF16)
        wffn = sbt("wffn", [128, NF, DM], BF16)
        wout_sb = sbt("wout_sb", [128, NF, DM], BF16)
        wqkv_j = sbt("wqkv_j", [128, 2, 3, NF, 128], BF16)
        qf = sbt("qf", [128, T], BF16)
        kf = sbt("kf", [128, T], BF16)
        vt = sbt("vt", [128, 16 * 130], BF16)
        bias_sb = qf[:, 0:2048].bitcast(F32)
        ffn_tmp = vt[:, 0:2048].bitcast(F32)
        msig = sbt("msig", [64, T], BF16)
        tanh_sb = sbt("tanh_sb", [64, 1024], BF16)
        ptm = sbt("ptm", [128, 2, 1024], BF16)
        pte2 = ptm
        att2 = sbt("att2", [128, 2, 256], BF16)
        rc2 = sbt("rc2", [128, 2, 4], F32)
        chv = sbt("chv", [64, 130], BF16)
        bdcv = sbt("bdcv", [64, 128], BF16)
        chk = sbt("chk", [128, 32], F32)
        cumk = sbt("cumk", [128, 32], BF16)
        glh = sbt("glh", [128, 64], BF16)
        mv = sbt("mv", [128, 32], F32)
        bnst = sbt("bnst", [128, 12], F32)
        rstd = sbt("rstd", [128, 16], F32)
        lnv = sbt("lnv", [128, 16], F32)
        b16 = sbt("b16", [1, DM], BF16)
        blo = sbt("blo", [1, DM], BF16)
        cmask_s = sbt("cmask_s", [128, 128], BF16)
        idn_s = sbt("idn_s", [128, 128], BF16)
        amask_s = sbt("amask_s", [128, 512], BF16)
        gm25_s = sbt("gm25_s", [64, T], BF16)
        big2_s = sbt("big2_s", [128, 160], BF16)
        tlow_s = sbt("tlow_s", [64, 64], BF16)
        ones2_s = sbt("ones2_s", [2, 128], BF16)
        onescol_s = sbt("onescol_s", [128, 1], BF16)
        epscol_s = sbt("epscol_s", [128, 1], F32)

        def carve_f32(lo_bf16, n_f32):
            return ln_tm[:, lo_bf16:lo_bf16 + 2 * n_f32].bitcast(F32)
        g1sb = carve_f32(0, 512).rearrange("p (c t b) -> p c t b", c=4, t=4)
        g1sbi = carve_f32(1024, 512).rearrange("p (c t b) -> p c t b", c=4, t=4)
        g2sb = carve_f32(2048, 64).rearrange("p (c e) -> p c e", c=4)
        g2sbi = carve_f32(2176, 64).rearrange("p (c e) -> p c e", c=4)
        waccR = carve_f32(2304, 512)
        waccI = carve_f32(3328, 512)
        wprod = carve_f32(4352, 512)
        wprod2 = carve_f32(7424, 512)
        bias_f = carve_f32(5376, 1024)[0:1, :]

        PW0 = _st.enter_context(nc.psum_tensor("PW0", [128, 1024], F32))
        PW1 = _st.enter_context(nc.psum_tensor("PW1", [128, 1024], F32))
        PSQ = _st.enter_context(nc.psum_tensor("PSQ", [128, 1024], F32))
        PB6 = _st.enter_context(nc.psum_tensor("PB6", [128, 512], F32))
        PB7 = _st.enter_context(nc.psum_tensor("PB7", [128, 512], F32))

        s_dma = Ctr(_st.enter_context(nc.semaphore("s_dma")))
        s_tr = [Ctr(_st.enter_context(nc.semaphore(f"s_tr{i}"))) for i in range(NT)]
        s_w = Ctr(_st.enter_context(nc.semaphore("s_w")))
        s_pe = Ctr(_st.enter_context(nc.semaphore("s_pe")))
        s_dve = Ctr(_st.enter_context(nc.semaphore("s_dve")))
        s_act = Ctr(_st.enter_context(nc.semaphore("s_act")))
        s_xl = [Ctr(_st.enter_context(nc.semaphore(f"s_xl{i}"))) for i in range(4)]
        s_g4 = Ctr(_st.enter_context(nc.semaphore("s_g4")))
        block = _st.enter_context(nc.Block())

        def _finish():
            @block.sync
            def _(sync):
                sync.wait_ge(s_dve.h, s_dve.n)
                sync.wait_ge(s_act.h, s_act.n)
                sync.wait_ge(s_pe.h, s_pe.n)
                sync.wait_ge(s_w.h, s_w.n)
                sync.wait_ge(s_dma.h, s_dma.n)
                sync.dma_start(out=y_out.rearrange("(tb p) c -> p tb c", p=128),
                               in_=x_tm[:].rearrange("p (tb c) -> p tb c", c=DM)).then_inc(s_dma.h, 16)
                s_dma.bump(16)
                if dbg_a is not None:
                    sync.dma_start(out=dbg_a, in_=ln_tm[:]).then_inc(s_dma.h, 16)
                    sync.dma_start(out=dbg_b, in_=ln_fm[:]).then_inc(s_dma.h, 16)
                    s_dma.bump(32)
                sync.wait_ge(s_dma.h, s_dma.n)

        @block.sync
        def _(sync):
            for ck in range(4):
                sync.dma_start(
                    out=x_tm[:, ck * 4 * DM:(ck + 1) * 4 * DM].rearrange("p (tb c) -> p tb c", c=DM),
                    in_=x_in.rearrange("(tb p) c -> p tb c", p=128)[:, 4 * ck:4 * ck + 4, :]) \
                    .then_inc(s_xl[ck].h, 16)
                s_xl[ck].bump(16)
            sync.dma_start(out=bias_f, in_=bias_cat).then_inc(s_dma.h, 16)
            s_dma.bump(16)
        p0_loads = s_dma.n

        @block.gpsimd
        def _(g):
            for src, dst in ((g1r.rearrange("c pt p b -> p c pt b"), g1sb),
                             (g2r.rearrange("c p e -> p c e"), g2sb),
                             (g1i.rearrange("c pt p b -> p c pt b"), g1sbi),
                             (g2i.rearrange("c p e -> p c e"), g2sbi)):
                g.dma_start(out=dst, in_=src).then_inc(s_g4.h, 16)
                s_g4.bump(16)
        g4_done = s_g4.n

        @block.gpsimd
        def _(g):
            for src, dst in ((cmask_d, cmask_s[:]), (idn_d, idn_s[:]), (amask_d, amask_s[:]),
                             (gm25_d, gm25_s[:]), (big2_d, big2_s[:]),
                             (tlow_d, tlow_s[:]), (ones2_d, ones2_s[:]), (onescol_d, onescol_s[:]),
                             (epscol_d, epscol_s[:])):
                g.dma_start(out=dst, in_=src).then_inc(s_w.h, 16)
                s_w.bump(16)
        g_loads_done = s_w.n

        def emit_stats(v, t):
            base = t * DM
            s_dve.inc(v.bn_stats(bnst[:, 0:6], x_tm[:, base:base + 512]))
            m2 = s_dve.inc(v.bn_stats(bnst[:, 6:12], x_tm[:, base + 512:base + 1024]))
            v.wait_ge(s_dve.h, m2)
            m3 = s_dve.inc(v.bn_aggr(mv[:, 2 * t:2 * t + 2], bnst[:]))
            v.wait_ge(s_dve.h, m3)
            return m3

        def emit_rstd(sc):
            var_ap = mv[:].rearrange("p (t two) -> p t two", two=2)[:, :, 1:2].squeeze(2)
            m = s_act.inc(sc.activation(lnv[:], var_ap, AF.Ln, bias=epscol_s[:], scale=1.0))
            sc.wait_ge(s_act.h, m)
            return s_act.inc(sc.activation(rstd[:], lnv[:], AF.Exp, scale=-0.5))

        def emit_normalize(v, t):
            base = t * DM
            return s_dve.inc(v.tensor_scalar(ln_tm[:, base:base + DM], x_tm[:, base:base + DM],
                                             mv[:, 2 * t:2 * t + 1], rstd[:, t:t + 1],
                                             AluOpType.subtract, AluOpType.mult))

        @block.vector
        def _(v):
            s_dve.inc(v.memset(glh[:], 0.0))
            v.wait_ge(s_g4.h, g4_done)
            for pt in range(4):
                ops = {}
                for tag, g1t, g2t, acc, scr in (("r", g1sb, g2sb, waccR, wprod),
                                                ("i", g1sbi, g2sbi, waccI, wprod2)):
                    seq = []
                    for c in range(4):
                        def prod(c=c, g1t=g1t, g2t=g2t, acc=acc, scr=scr, pt=pt):
                            a_bc = g1t[:, c, pt, :].unsqueeze(2).broadcast_to((128, 32, 16))
                            b_bc = g2t[:, c, :].unsqueeze(1).broadcast_to((128, 32, 16))
                            dst = acc if c == 0 else scr
                            return v.tensor_tensor(
                                dst.rearrange("p (b e) -> p b e", e=16), a_bc, b_bc,
                                AluOpType.mult)
                        seq.append(prod)
                        if c > 0:
                            seq.append(lambda acc=acc, scr=scr: v.tensor_tensor(
                                acc, acc, scr, AluOpType.add))
                    ops[tag] = seq
                marks = {"r": 0, "i": 0}
                for tag, emit in [z for pair in zip(
                        [("r", o) for o in ops["r"]],
                        [("i", o) for o in ops["i"]]) for z in pair]:
                    if marks[tag]:
                        v.wait_ge(s_dve.h, marks[tag])
                    marks[tag] = s_dve.inc(emit())
                v.wait_ge(s_dve.h, marks["r"])
                s_dve.inc(v.tensor_copy(wffn[:, pt, 0:512], waccR))
                s_dve.inc(v.tensor_copy(wffn[:, pt + 4, 512:1024], waccR))
                v.wait_ge(s_dve.h, marks["i"])
                s_dve.inc(v.tensor_copy(wffn[:, pt, 512:1024], waccI))
                m4 = s_dve.inc(v.tensor_scalar(wffn[:, pt + 4, 0:512], waccI, -1.0, None,
                                               AluOpType.mult))
                v.wait_ge(s_dve.h, m4)
                v.wait_ge(s_xl[pt].h, 16)
                for t in range(4 * pt, 4 * pt + 4):
                    emit_stats(v, t)
        ln1_stats = s_dve.n

        @block.vector
        def _(v):
            v.wait_ge(s_dma.h, p0_loads)
            m16 = s_dve.inc(v.tensor_copy(b16[:], bias_f))
            v.wait_ge(s_dve.h, m16)
            s_dve.inc(v.tensor_tensor(blo[:], bias_f, b16[:], AluOpType.subtract))
            s_dve.inc(v.memset(bdcv[:], 0.0))
            s_dve.inc(v.memset(cumk[:, 0:1], 0.0))
        wffn_done = s_dve.n

        @block.tensor
        def _(te):
            for k in range(1, ln1_stats, 3):
                te.wait_ge(s_dve.h, k)
                te.matmul(PB7[0:1, 0:16], glh[0:1, 0:1],
                          glh[0:1, 0:1].broadcast_to((1, 16)),
                          start=True, stop=True)

        @block.tensor
        def _(te):
            te.wait_ge(s_dve.h, wffn_done)
            te.wait_ge(s_w.h, g_loads_done)
            for hw in range(2):
                te.matmul(PW0[:, hw * 512:hw * 512 + 512], ones2_s[0:1, :],
                          b16[:, hw * 512:hw * 512 + 512], start=True, stop=False)
                mm = te.matmul(PW0[:, hw * 512:hw * 512 + 512], ones2_s[0:1, :],
                               blo[:, hw * 512:hw * 512 + 512], start=False, stop=True)
            s_pe.inc(mm)
        bias_pe = s_pe.n

        @block.vector
        def _(v):
            v.wait_ge(s_pe.h, bias_pe)
            s_dve.inc(v.tensor_copy(bias_sb[:], PW0[:]))
        bias_evac = s_dve.n

        def emit_transposes(wait_dve_at, wait_w_at):
            marks = []

            @block.sync
            def _(sync):
                if not isinstance(wait_dve_at, list):
                    sync.wait_ge(s_dve.h, wait_dve_at)
                if wait_w_at:
                    sync.wait_ge(s_w.h, wait_w_at)
                sync.wait_ge(s_dma.h, s_dma.n)
                for t in range(NT):
                    if isinstance(wait_dve_at, list):
                        sync.wait_ge(s_dve.h, wait_dve_at[t])
                    dst = ln_fm[:].rearrange("p (f tt) -> p f tt", tt=T)[:, :, t * 128:(t + 1) * 128]
                    sync.dma_start_transpose(dst, ln_tm[:, t * DM:(t + 1) * DM]).then_inc(s_tr[t].h, 16)
                    marks.append(s_tr[t].bump(16))
            return marks

        def emit_ln_tail(wait_dve_at, wait_w_at):
            @block.scalar
            def _(sc):
                sc.wait_ge(s_dve.h, wait_dve_at)
                sc.wait_ge(s_w.h, g_loads_done)
                emit_rstd(sc)
            rstd_mark = s_act.n

            norm_marks = []

            @block.vector
            def _(v):
                v.wait_ge(s_act.h, rstd_mark)
                for t in range(NT):
                    norm_marks.append(emit_normalize(v, t))
            return emit_transposes(norm_marks, wait_w_at)

        def emit_mlp(kind, wait_dma_at, with_stats, extra_pe_wait=0, marks_out=None,
                     bias_at=0):
            dve_marks = []
            if marks_out is not None:
                dve_marks = marks_out
            for tb in range(NT):
                @block.tensor
                def _(te, tb=tb):
                    if isinstance(wait_dma_at, list):
                        te.wait_ge(s_tr[tb].h, wait_dma_at[tb])
                    elif tb == 0:
                        te.wait_ge(s_dma.h, wait_dma_at)
                    if tb == 0:
                        if kind == "ffn":
                            te.wait_ge(s_dve.h, wffn_done)
                            te.wait_ge(s_dve.h, bias_evac)
                    if tb == 0 and kind != "ffn":
                        te.wait_ge(s_w.h, extra_pe_wait)
                    if tb >= 2:
                        te.wait_ge(s_dve.h, dve_marks[tb - 2])
                    ps = PW0 if tb % 2 == 0 else PW1
                    wmat = wffn if kind == "ffn" else wout_sb
                    for f in range(NF):
                        lhsT = ln_fm[:, f * T + tb * 128: f * T + tb * 128 + 128]
                        first = f == 0
                        last = f == NF - 1
                        te.matmul(ps[:, 0:512], lhsT, wmat[:, f, 0:512], start=first, stop=last)
                        mm = te.matmul(ps[:, 512:1024], lhsT, wmat[:, f, 512:1024],
                                       start=first, stop=last)
                    s_pe.inc(mm)
                pe_mark = s_pe.n

                @block.vector
                def _(v, tb=tb, pe_mark=pe_mark):
                    v.wait_ge(s_pe.h, pe_mark)
                    ps = PW0 if tb % 2 == 0 else PW1
                    base = tb * DM
                    if kind == "ffn":
                        if bias_at:
                            v.wait_ge(s_dve.h, bias_at)
                        if dve_marks:
                            v.wait_ge(s_dve.h, dve_marks[-1])
                        mt = s_dve.inc(v.tensor_tensor(
                            ffn_tmp[:], ps[:], bias_sb[:], AluOpType.add))
                        v.wait_ge(s_dve.h, mt)
                        m = s_dve.inc(v.scalar_tensor_tensor(
                            x_tm[:, base:base + DM], ffn_tmp[:], 0.5,
                            x_tm[:, base:base + DM], AluOpType.mult, AluOpType.add))
                    else:
                        m = s_dve.inc(v.tensor_tensor(
                            x_tm[:, base:base + DM], ps[:],
                            x_tm[:, base:base + DM], AluOpType.add))
                    dve_marks.append(m)
                    if with_stats:
                        v.wait_ge(s_dve.h, m)
                        emit_stats(v, tb)
            return s_dve.n

        ln1_T = emit_ln_tail(wffn_done, g_loads_done)
        if stop == "ln1":
            _finish()
            return nc

        ffn1_done = emit_mlp("ffn", ln1_T, with_stats=True, bias_at=bias_evac)

        @block.vector
        def _(v):
            v.wait_ge(s_dve.h, ffn1_done)
            s_dve.inc(v.memset(vt[:].rearrange("p (blk c) -> p blk c", c=130)[:, :, 64:65], 1.0))
            s_dve.inc(v.memset(vt[:].rearrange("p (blk c) -> p blk c", c=130)[:, :, 129:130], 1.0))
        if stop == "ffn1":
            _finish()
            return nc

        wq_marks = []

        @block.gpsimd
        def _(g):
            for t_ in range(NT):
                g.wait_ge(s_tr[t_].h, ln1_T[t_])
            for j0 in range(2):
                if j0 > 0:
                    g.wait_ge(s_w.h, wq_marks[-1])
                src_ = w_qkv.rearrange("(fc p) (w c) -> p w fc c", p=128, w=3)[:, :, :, 128 * j0:128 * j0 + 128]
                g.dma_start(out=wqkv_j[:, j0], in_=src_).then_inc(s_w.h, 16)
                wq_marks.append(s_w.bump(16))

        ln2_T = emit_ln_tail(ffn1_done, s_w.n)
        if stop == "ln2":
            _finish()
            return nc

        pe_v_done = []
        prev_j_exp3 = 0
        prev_j_exp2 = 0
        prev_j_evac15 = 0
        last_evac = 0
        wout_loaded = 0

        for j in range(NJ):
            slot = j % 2

            qk_evac_marks = []
            for ii in range(4):
                part = 1 - ii // 2

                @block.tensor
                def _(te, j=j, slot=slot, ii=ii, part=part):
                    if ii == 0:
                        te.wait_ge(s_w.h, wq_marks[j])
                        if j == 0:
                            for t_ in range(NT):
                                te.wait_ge(s_tr[t_].h, ln2_T[t_])
                        else:
                            te.wait_ge(s_act.h, prev_j_exp2)
                    if ii == 1 and j > 0:
                        te.wait_ge(s_act.h, prev_j_exp3)
                    ps = PSQ if ii % 2 == 0 else PW1
                    if ii >= 2:
                        te.wait_ge(s_act.h, qk_evac_marks[ii - 2])
                    for half in range(2):
                        tq = (ii % 2) * 2 + half
                        for f in range(NF):
                            mm = te.matmul(ps[:, half * 512:half * 512 + 512],
                                           wqkv_j[:, slot, part, f, :],
                                           ln_fm[:, f * T + tq * 512: f * T + (tq + 1) * 512],
                                           start=(f == 0), stop=(f == NF - 1))
                        s_pe.inc(mm)
                pe_mark = s_pe.n

                @block.scalar
                def _(sc, ii=ii, part=part, pe_mark=pe_mark):
                    sc.wait_ge(s_pe.h, pe_mark)
                    ps = PSQ if ii % 2 == 0 else PW1
                    dst = qf if part == 0 else kf
                    s_act.inc(sc.copy(dst[:, (ii % 2) * 1024:(ii % 2) * 1024 + 1024], ps[:]))
                qk_evac_marks.append(s_act.n)

            @block.vector
            def _(v, j=j):
                def step(instr):
                    m = s_dve.inc(instr)
                    v.wait_ge(s_dve.h, m)
                v.wait_ge(s_act.h, qk_evac_marks[1])
                step(v.tensor_reduce(chk[:].unsqueeze(2),
                                     kf[:].rearrange("p (c w) -> p c w", w=64),
                                     axis=AX.X, op=AluOpType.add))
                step(v.tensor_tensor_scan(cumk[:, 1:32], chk[:, 0:31], chk[:, 0:31], 0.0,
                                          AluOpType.add, AluOpType.bypass))
                step(v.tensor_copy(glh[0:64, 0:32], cumk[0:64, :]))
                step(v.tensor_copy(glh[64:128, 32:64], cumk[64:128, :]))
            glh_dve = s_dve.n

            v_evac_marks = []

            def emit_v_phase():
              for tbq in range(4):
                @block.tensor
                def _(te, j=j, slot=slot, tbq=tbq):
                    ps = PB6 if tbq % 2 == 0 else PB7
                    if tbq >= 2:
                        te.wait_ge(s_dve.h, v_evac_marks[tbq - 2])
                    elif j > 0:
                        te.wait_ge(s_dve.h, prev_j_evac15)
                    for tb in range(4 * tbq, 4 * tbq + 4):
                        for f in range(NF):
                            mm = te.matmul(ps[:, (tb % 4) * 128:(tb % 4) * 128 + 128],
                                           ln_fm[:, f * T + tb * 128: f * T + tb * 128 + 128],
                                           wqkv_j[:, slot, 2, f, :],
                                           start=(f == 0), stop=(f == NF - 1))
                        s_pe.inc(mm)
                pe_mark = s_pe.n
                if tbq == 3:
                    pe_v_done.append(pe_mark)

                @block.vector
                def _(v, tbq=tbq, pe_mark=pe_mark):
                    v.wait_ge(s_pe.h, pe_mark)
                    ps = PB6 if tbq % 2 == 0 else PB7
                    dst = vt[:, tbq * 4 * 130:(tbq + 1) * 4 * 130].rearrange(
                        "p (b c) -> p b c", c=130).rearrange(
                        "p b (h c2) -> p b h c2", c2=65)[:, :, :, 0:64]
                    s_dve.inc(v.tensor_copy(
                        dst, ps[:].rearrange("p (b h d) -> p b h d", h=2, d=64)))
                v_evac_marks.append(s_dve.n)

            def emit_aprime():
              nonlocal wout_loaded
              if j + 2 <= NJ - 1 or j == NJ - 1:
                @block.gpsimd
                def _(g, j=j):
                    if j + 2 <= NJ - 1:
                        jl = j + 2
                        g.wait_ge(s_w.h, wq_marks[jl - 1])
                        g.wait_ge(s_pe.h, pe_v_done[j])
                        src_ = w_qkv.rearrange("(fc p) (w c) -> p w fc c", p=128, w=3)[:, :, :, 128 * jl:128 * jl + 128]
                        g.dma_start(out=wqkv_j[:, jl % 2], in_=src_).then_inc(s_w.h, 16)
                        wq_marks.append(s_w.bump(16))
                    if j == NJ - 1:
                        g.wait_ge(s_w.h, wq_marks[-1])
                        g.dma_start(out=wout_sb[:],
                                    in_=w_out.rearrange("(fc p) c -> p fc c", p=128)).then_inc(s_w.h, 16)
                        s_w.bump(16)
              if j == NJ - 1:
                wout_loaded = s_w.n


            def emit_chv():
                @block.tensor
                def _(te):
                    te.wait_ge(s_dve.h, v_evac_marks[3])
                    for p in range(NP):
                        mm = te.matmul(PB6[0:64, 0:130], big2_s[:, 32 - 2 * p:96 - 2 * p],
                                       vt[:, p * 130:(p + 1) * 130],
                                       start=(p == 0), stop=(p == NP - 1))
                    s_pe.inc(mm)
                chv_pe = s_pe.n

                @block.vector
                def _(v, chv_pe=chv_pe):
                    v.wait_ge(s_pe.h, chv_pe)
                    s_dve.inc(v.tensor_copy(chv[:], PB6[0:64, 0:130]))
                return s_dve.n

            def emit_tlow(chv_dve):
                @block.tensor
                def _(te):
                    te.wait_ge(s_dve.h, chv_dve)
                    s_pe.inc(te.matmul(PB6[0:64, 256:386], tlow_s[:], chv[:],
                                       start=True, stop=True))
                return s_pe.n

            def emit_bdcv(cumv_pe):
                @block.vector
                def _(v):
                    def step(instr):
                        m = s_dve.inc(instr)
                        v.wait_ge(s_dve.h, m)
                    v.wait_ge(s_pe.h, cumv_pe)
                    step(v.tensor_copy(bdcv[0:32, 0:64], PB6[0:32, 256:320]))
                    step(v.tensor_copy(bdcv[32:64, 64:128], PB6[32:64, 321:385]))
                return s_dve.n

            sgrp_pe = []
            exp_marks = []
            mask_marks = []
            pv_pe = []
            evac_mult = []
            evac_done = []
            tanh_marks = []
            msig_marks = []
            gate_pe_marks = []

            def emit_sgrp_pe(q, j=j):
                @block.tensor
                def _(te, q=q):
                    if q <= 1:
                        te.wait_ge(s_act.h, qk_evac_marks[2 + q])
                    else:
                        te.wait_ge(s_act.h, exp_marks[q - 2])
                    bank = PSQ if q % 2 == 0 else PW1
                    for pp in range(4):
                        p = 4 * q + pp
                        mmA = te.matmul(bank[:, pp * 128:pp * 128 + 128],
                                        kf[0:64, p * 128:(p + 1) * 128],
                                        qf[0:64, p * 128:(p + 1) * 128], start=True, stop=True)
                        mmB = te.matmul(bank[:, 512 + pp * 128:512 + pp * 128 + 128],
                                        kf[64:128, p * 128:(p + 1) * 128],
                                        qf[64:128, p * 128:(p + 1) * 128], start=True, stop=True)
                        if pp == 3:
                            s_pe.inc(mmA)
                            s_pe.inc(mmB)
                sgrp_pe.append(s_pe.n)

            def emit_sgrp_post(q, j=j):
                @block.scalar
                def _(sc, q=q):
                    sc.wait_ge(s_pe.h, sgrp_pe[q])
                    if q >= 2:
                        sc.wait_ge(s_pe.h, pv_pe[q - 2])
                        sc.wait_ge(s_dve.h, mask_marks[q - 2])
                    bank = PSQ if q % 2 == 0 else PW1
                    s_act.inc(sc.activation(pte2[:, q % 2, :], bank[:], AF.Exp, scale=0.125))
                exp_marks.append(s_act.n)

                @block.vector
                def _(v, q=q):
                    v.wait_ge(s_act.h, exp_marks[q])
                    cm = cmask_s[:].unsqueeze(1).broadcast_to((128, 8, 128))
                    s_dve.inc(v.tensor_tensor(
                        ptm[:, q % 2, :].rearrange("p (a b) -> p a b", b=128),
                        pte2[:, q % 2, :].rearrange("p (a b) -> p a b", b=128),
                        cm, AluOpType.mult))
                mask_marks.append(s_dve.n)

            def emit_gate_pe(sbp, j=j):
                if True:
                    @block.tensor
                    def _(te, sbp=sbp):
                        if sbp == 0:
                            te.wait_ge(s_dve.h, glh_dve)
                        else:
                            te.wait_ge(s_act.h, tanh_marks[sbp - 1])
                        te.wait_ge(s_act.h, qk_evac_marks[2 + sbp])
                        for half in range(2):
                            sbi = sbp * 2 + half
                            mm = te.matmul(PW0[0:64, half * 512:half * 512 + 512],
                                           glh[:, 0:64],
                                           qf[:, sbi * 512:(sbi + 1) * 512], start=True, stop=True)
                            s_pe.inc(mm)
                    gate_pe_marks.append(s_pe.n)

            def emit_gate_post(sbp, j=j):
                if True:
                    pe_mark = gate_pe_marks[sbp]

                    @block.scalar
                    def _(sc, sbp=sbp, pe_mark=pe_mark):
                        sc.wait_ge(s_pe.h, pe_mark)
                        if sbp >= 1:
                            sc.wait_ge(s_dve.h, msig_marks[sbp - 1])
                        s_act.inc(sc.activation(tanh_sb[:], PW0[0:64, :], AF.Tanh, scale=TANH_SCALE))
                    tanh_marks.append(s_act.n)

                    @block.vector
                    def _(v, sbp=sbp):
                        v.wait_ge(s_act.h, tanh_marks[sbp])
                        s_dve.inc(v.scalar_tensor_tensor(
                            msig[:, sbp * 1024:(sbp + 1) * 1024], tanh_sb[:], 1.0,
                            gm25_s[:, sbp * 1024:(sbp + 1) * 1024], AluOpType.add, AluOpType.mult))
                    msig_marks.append(s_dve.n)

            def emit_pv_group(q, bdcv_at=0, j=j):
                @block.tensor
                def _(te, q=q, bdcv_at=bdcv_at):
                    te.wait_ge(s_dve.h, mask_marks[q])
                    if q == 0:
                        te.wait_ge(s_dve.h, msig_marks[0])
                        te.wait_ge(s_act.h, tanh_marks[1])
                        te.wait_ge(s_dve.h, bdcv_at)
                        if j > 0:
                            te.wait_ge(s_dve.h, prev_j_evac15)
                    else:
                        te.wait_ge(s_dve.h, evac_mult[q - 1])
                    if q == 2:
                        te.wait_ge(s_dve.h, msig_marks[1])
                    slotbase = (0, 130, 512, 642)
                    for pp in range(4):
                        p = 4 * q + pp
                        base = slotbase[pp]
                        ptA = ptm[:, q % 2, pp * 128:pp * 128 + 128]
                        ptB = ptm[:, q % 2, 512 + pp * 128:512 + pp * 128 + 128]
                        te.matmul(PW0[:, base:base + 65], ptA,
                                  vt[:, p * 130:p * 130 + 65], start=True, stop=True)
                        mmD = te.matmul(PW0[:, base + 65:base + 130], ptB,
                                        vt[:, p * 130 + 65:p * 130 + 130],
                                        start=True, stop=True)
                    s_pe.inc(mmD)

                    if q >= 1:
                        te.wait_ge(s_dve.h, evac_done[q - 1])
                    for pp in range(4):
                        p = 4 * q + pp
                        cr = PB6 if pp < 2 else PB7
                        mmC = te.matmul(cr[:, (pp % 2) * 128:(pp % 2) * 128 + 128],
                                        msig[:, p * 128:(p + 1) * 128], bdcv[:],
                                        start=True, stop=True)
                    s_pe.inc(mmC)
                pv_pe.append(s_pe.n)

                @block.vector
                def _(v, q=q, j=j):
                    v.wait_ge(s_pe.h, pv_pe[q])
                    marks = []
                    for half in range(2):
                        pvh = PW0[:, half * 512:half * 512 + 260].rearrange(
                            "p (s c2) -> p s c2", c2=65)
                        marks.append(s_dve.inc(v.reciprocal(
                            rc2[:, half, :], pvh[:, :, 64:65].squeeze(2))))
                    for half in range(2):
                        v.wait_ge(s_dve.h, marks[half])
                        pvh = PW0[:, half * 512:half * 512 + 260].rearrange(
                            "p (s c2) -> p s c2", c2=65)
                        marks.append(s_dve.inc(v.tensor_tensor(
                            att2[:, half, :].rearrange("p (s d) -> p s d", d=64),
                            pvh[:, :, 0:64],
                            rc2[:, half, :].unsqueeze(2).broadcast_to((128, 4, 64)),
                            AluOpType.mult)))
                    evac_mult.append(s_dve.n)
                    for half in range(2):
                        v.wait_ge(s_dve.h, marks[2 + half])
                        crb = PB6 if half == 0 else PB7
                        t0 = 4 * q + 2 * half
                        s_dve.inc(v.tensor_tensor(
                            ln_tm[:].rearrange("p (t c) -> p t c", c=DM)[
                                :, t0:t0 + 2, j * 128:j * 128 + 128],
                            crb[:, 0:256].rearrange("p (s c) -> p s c", c=128),
                            att2[:, half, :].rearrange("p (s d) -> p s d", d=128),
                            AluOpType.add))
                evac_done.append(s_dve.n)

            emit_sgrp_pe(0)
            emit_gate_pe(0)
            emit_v_phase()
            emit_aprime()
            emit_sgrp_post(0)
            emit_gate_post(0)
            emit_sgrp_pe(1)
            emit_gate_pe(1)
            chv_dve = emit_chv()
            cumv_pe = emit_tlow(chv_dve)
            bdcv_dve = emit_bdcv(cumv_pe)
            emit_sgrp_post(1)
            emit_gate_post(1)
            emit_pv_group(0, bdcv_dve)
            emit_sgrp_pe(2)
            emit_sgrp_post(2)
            emit_pv_group(1)
            emit_sgrp_pe(3)
            emit_sgrp_post(3)
            emit_pv_group(2)
            emit_pv_group(3)

            prev_j_exp3 = exp_marks[3]
            prev_j_exp2 = exp_marks[2]
            prev_j_evac15 = evac_done[-1]
            last_evac = evac_done[-1]

        if stop == "attn":
            _finish()
            return nc

        @block.tensor
        def _(te):
            te.wait_ge(s_act.h, prev_j_exp3)
            te.wait_ge(s_act.h, s_act.n)
            te.wait_ge(s_dve.h, last_evac)
            for hw in range(2):
                te.matmul(PSQ[:, hw * 512:hw * 512 + 512], ones2_s[0:1, :],
                          b16[:, hw * 512:hw * 512 + 512], start=True, stop=False)
                mm = te.matmul(PSQ[:, hw * 512:hw * 512 + 512], ones2_s[0:1, :],
                               blo[:, hw * 512:hw * 512 + 512], start=False, stop=True)
            s_pe.inc(mm)
        bias2_pe = s_pe.n

        @block.vector
        def _(v):
            v.wait_ge(s_pe.h, bias2_pe)
            s_dve.inc(v.tensor_copy(bias_sb[:], PSQ[:]))
        bias_evac2 = s_dve.n

        attn_T = emit_transposes(last_evac, wout_loaded)

        wout_done = emit_mlp("wout", attn_T, with_stats=True, extra_pe_wait=wout_loaded)
        if stop == "wout":
            _finish()
            return nc

        ln3_T = emit_ln_tail(wout_done, s_w.n)

        ffn2_marks = []
        ffn2_done = emit_mlp("ffn", ln3_T, with_stats=False, marks_out=ffn2_marks,
                             bias_at=bias_evac2)

        @block.sync
        def _(sync):
            for tb in range(NT):
                sync.wait_ge(s_dve.h, ffn2_marks[tb])
                sync.dma_start(out=y_out.rearrange("(tb p) c -> p tb c", p=128)[:, tb:tb + 1, :],
                               in_=x_tm[:, tb * DM:(tb + 1) * DM].unsqueeze(1)).then_inc(s_dma.h, 16)
                s_dma.bump(16)
            sync.wait_ge(s_dma.h, s_dma.n)

    return nc



def _consts():
    cmask = np.zeros((128, 128), np.float32)
    s2g, s1g = np.meshgrid(np.arange(128), np.arange(128), indexing="ij")
    cmask[(s2g // 64 == s1g // 64) & (s2g % 64 <= s1g % 64)] = 1.0
    gm25 = np.zeros((64, T), np.float32)
    ch = (np.arange(T) // 64)
    for r in range(64):
        gm25[r, :] = np.where((r % 32) == ch, 0.25, 0.0)
    big2 = np.zeros((128, 160), np.float32)
    tvec = np.arange(128)
    big2[tvec < 64, 32] = 1.0
    big2[tvec >= 64, 33] = 1.0
    big2[tvec < 64, 64] = 1.0
    big2[tvec >= 64, 65] = 1.0
    tlow = np.zeros((64, 64), np.float32)
    for c in range(32):
        for cp in range(32):
            if c < cp:
                tlow[c, cp] = 1.0 / 64.0
                tlow[32 + c, 32 + cp] = 1.0 / 64.0
    ones2 = np.ones((2, 128), np.float32)
    onescol = np.ones((128, 1), np.float32)
    epscol = np.full((128, 1), EPS, np.float32)
    idn = np.eye(128, dtype=np.float32)
    amask = np.where(np.tile(cmask, (1, 4)) > 0, 0.0, -16384.0).astype(np.float32)
    return cmask, gm25, big2, tlow, ones2, onescol, epscol, idn, amask


def _expand_g1(G1):
    out = np.empty((4, 4, 128, 32), np.float32)
    for c in range(4):
        for pt in range(4):
            d = 8 * pt + (np.arange(128) // 16)
            out[c, pt] = G1[:, c, :][:, d].T
    return out


def _expand_g2(G2):
    out = np.empty((4, 128, 16), np.float32)
    g = np.arange(128) % 16
    for c in range(4):
        out[c] = G2[c, :, :].T[g, :]
    return out


def kernel(L0_attn_x, L0_attn_W_qkv, L0_attn_W_out, L0_ffn_G1_R, L0_ffn_G2_R,
           L0_ffn_G1_I, L0_ffn_G2_I, L0_ffn_bias_real, L0_ffn_bias_imag):
    B = L0_attn_x.shape[0]
    assert B == 8 and L0_attn_x.shape[1] == T and L0_attn_x.shape[2] == DM

    G1R = np.asarray(L0_ffn_G1_R, np.float32)[0]
    G1I = np.asarray(L0_ffn_G1_I, np.float32)[0]
    G2R = np.asarray(L0_ffn_G2_R, np.float32)[:, :, 0, :]
    G2I = np.asarray(L0_ffn_G2_I, np.float32)[:, :, 0, :]
    g1r_e = _expand_g1(G1R)
    g1i_e = _expand_g1(G1I)
    g2r_e = _expand_g2(G2R)
    g2i_e = _expand_g2(G2I)
    bias_cat = np.concatenate([np.asarray(L0_ffn_bias_real, np.float32),
                               np.asarray(L0_ffn_bias_imag, np.float32)])[None, :]
    cmask, gm25, big2, tlow, ones2, onescol, epscol, idn, amask = _consts()

    wqkv = np.ascontiguousarray(np.asarray(L0_attn_W_qkv, np.float32))
    wout = np.ascontiguousarray(np.asarray(L0_attn_W_out, np.float32))
    x = np.ascontiguousarray(np.asarray(L0_attn_x, np.float32))

    nc = build_program()
    in_maps = []
    for b in range(B):
        in_maps.append(dict(
            x_in=x[b], w_qkv=wqkv, w_out=wout,
            g1r=g1r_e, g2r=g2r_e, g1i=g1i_e, g2i=g2i_e,
            bias_cat=bias_cat, cmask=cmask, gm25=gm25, big2=big2,
            tlow=tlow, ones2=ones2, onescol=onescol, epscol=epscol,
            idn=idn, amask=amask,
        ))
    res = run_bass_kernel_spmd(nc, in_maps, core_ids=list(range(B)))
    out = np.stack([np.asarray(res.results[b]["y_out"]) for b in range(B)], axis=0)
    return out.astype(np.float32)


# version 27 (final)
# speedup vs baseline: 1.2011x; 1.2011x over previous
import numpy as np
import concourse.bass as bass
import concourse.mybir as mybir
from concourse.bass_utils import run_bass_kernel_spmd
from concourse.alu_op_type import AluOpType

F32 = mybir.dt.float32
BF16 = mybir.dt.bfloat16
AF = mybir.ActivationFunctionType
AX = mybir.AxisListType

T = 2048
DM = 1024
NT = 16
NF = 8
NJ = 8
NP = 16
EPS = 1e-5
TANH_SCALE = 0.125 / (64.0 * 2.0)


class Ctr:
    def __init__(self, h):
        self.h = h
        self.n = 0

    def bump(self, k=1):
        self.n += k
        return self.n

    def inc(self, instr, k=1):
        instr.then_inc(self.h, k)
        return self.bump(k)


def build_program(stop="full"):
    nc = bass.Bass("TRN2", target_bir_lowering=False)

    x_in = nc.dram_tensor("x_in", [T, DM], F32, kind="ExternalInput").ap()
    w_qkv = nc.dram_tensor("w_qkv", [DM, 3 * DM], F32, kind="ExternalInput").ap()
    w_out = nc.dram_tensor("w_out", [DM, DM], F32, kind="ExternalInput").ap()
    g1r = nc.dram_tensor("g1r", [4, 4, 128, 32], F32, kind="ExternalInput").ap()
    g2r = nc.dram_tensor("g2r", [4, 128, 16], F32, kind="ExternalInput").ap()
    g1i = nc.dram_tensor("g1i", [4, 4, 128, 32], F32, kind="ExternalInput").ap()
    g2i = nc.dram_tensor("g2i", [4, 128, 16], F32, kind="ExternalInput").ap()
    bias_cat = nc.dram_tensor("bias_cat", [1, DM], F32, kind="ExternalInput").ap()
    cmask_d = nc.dram_tensor("cmask", [128, 128], F32, kind="ExternalInput").ap()
    idn_d = nc.dram_tensor("idn", [128, 128], F32, kind="ExternalInput").ap()
    amask_d = nc.dram_tensor("amask", [128, 512], F32, kind="ExternalInput").ap()
    gm25_d = nc.dram_tensor("gm25", [64, T], F32, kind="ExternalInput").ap()
    big2_d = nc.dram_tensor("big2", [128, 160], F32, kind="ExternalInput").ap()
    tlow_d = nc.dram_tensor("tlow", [64, 64], F32, kind="ExternalInput").ap()
    ones2_d = nc.dram_tensor("ones2", [2, 128], F32, kind="ExternalInput").ap()
    onescol_d = nc.dram_tensor("onescol", [128, 1], F32, kind="ExternalInput").ap()
    epscol_d = nc.dram_tensor("epscol", [128, 1], F32, kind="ExternalInput").ap()
    y_out = nc.dram_tensor("y_out", [T, DM], F32, kind="ExternalOutput").ap()
    dbg_a = dbg_b = None
    if stop != "full":
        dbg_a = nc.dram_tensor("dbg_a", [128, NT * DM], BF16, kind="ExternalOutput").ap()
        dbg_b = nc.dram_tensor("dbg_b", [128, NF * T], BF16, kind="ExternalOutput").ap()

    from contextlib import ExitStack
    with ExitStack() as _st:
        def sbt(name, shape, dt):
            return _st.enter_context(nc.sbuf_tensor(name, shape, dt))

        x_tm = sbt("x_tm", [128, NT * DM], F32)
        ln_tm = sbt("ln_tm", [128, NT * DM], BF16)
        ln_fm = sbt("ln_fm", [128, NF * T], BF16)
        wffn = sbt("wffn", [128, NF, DM], BF16)
        wout_sb = sbt("wout_sb", [128, NF, DM], BF16)
        wqkv_j = sbt("wqkv_j", [128, 2, 3, NF, 128], BF16)
        qf = sbt("qf", [128, T], BF16)
        kf = sbt("kf", [128, T], BF16)
        vt = sbt("vt", [128, 16 * 130], BF16)
        bias_sb = qf[:, 0:2048].bitcast(F32)
        ffn_tmp = vt[:, 0:2048].bitcast(F32)
        msig = sbt("msig", [64, T], BF16)
        tanh_sb = sbt("tanh_sb", [64, 1024], BF16)
        ptm = sbt("ptm", [128, 2, 1024], BF16)
        pte2 = ptm
        att2 = sbt("att2", [128, 2, 256], BF16)
        rc2 = sbt("rc2", [128, 2, 4], F32)
        chv = sbt("chv", [64, 130], BF16)
        bdcv = sbt("bdcv", [64, 128], BF16)
        chk = sbt("chk", [128, 32], F32)
        cumk = sbt("cumk", [128, 32], BF16)
        glh = sbt("glh", [128, 64], BF16)
        mv = sbt("mv", [128, 32], F32)
        bnst = sbt("bnst", [128, 12], F32)
        rstd = sbt("rstd", [128, 16], F32)
        lnv = sbt("lnv", [128, 16], F32)
        b16 = sbt("b16", [1, DM], BF16)
        blo = sbt("blo", [1, DM], BF16)
        cmask_s = sbt("cmask_s", [128, 128], BF16)
        idn_s = sbt("idn_s", [128, 128], BF16)
        amask_s = sbt("amask_s", [128, 512], BF16)
        gm25_s = sbt("gm25_s", [64, T], BF16)
        big2_s = sbt("big2_s", [128, 160], BF16)
        tlow_s = sbt("tlow_s", [64, 64], BF16)
        ones2_s = sbt("ones2_s", [2, 128], BF16)
        onescol_s = sbt("onescol_s", [128, 1], BF16)
        epscol_s = sbt("epscol_s", [128, 1], F32)

        def carve_f32(lo_bf16, n_f32):
            return ln_tm[:, lo_bf16:lo_bf16 + 2 * n_f32].bitcast(F32)
        g1sb = carve_f32(0, 512).rearrange("p (c t b) -> p c t b", c=4, t=4)
        g1sbi = carve_f32(1024, 512).rearrange("p (c t b) -> p c t b", c=4, t=4)
        g2sb = carve_f32(2048, 64).rearrange("p (c e) -> p c e", c=4)
        g2sbi = carve_f32(2176, 64).rearrange("p (c e) -> p c e", c=4)
        waccR = carve_f32(2304, 512)
        waccI = carve_f32(3328, 512)
        wprod = carve_f32(4352, 512)
        wprod2 = carve_f32(7424, 512)
        bias_f = carve_f32(5376, 1024)[0:1, :]

        PW0 = _st.enter_context(nc.psum_tensor("PW0", [128, 1024], F32))
        PW1 = _st.enter_context(nc.psum_tensor("PW1", [128, 1024], F32))
        PSQ = _st.enter_context(nc.psum_tensor("PSQ", [128, 1024], F32))
        PB6 = _st.enter_context(nc.psum_tensor("PB6", [128, 512], F32))
        PB7 = _st.enter_context(nc.psum_tensor("PB7", [128, 512], F32))

        s_dma = Ctr(_st.enter_context(nc.semaphore("s_dma")))
        s_tr = [Ctr(_st.enter_context(nc.semaphore(f"s_tr{i}"))) for i in range(NT)]
        s_w = Ctr(_st.enter_context(nc.semaphore("s_w")))
        s_pe = Ctr(_st.enter_context(nc.semaphore("s_pe")))
        s_dve = Ctr(_st.enter_context(nc.semaphore("s_dve")))
        s_act = Ctr(_st.enter_context(nc.semaphore("s_act")))
        s_xl = [Ctr(_st.enter_context(nc.semaphore(f"s_xl{i}"))) for i in range(4)]
        s_g4 = Ctr(_st.enter_context(nc.semaphore("s_g4")))
        block = _st.enter_context(nc.Block())

        def _finish():
            @block.sync
            def _(sync):
                sync.wait_ge(s_dve.h, s_dve.n)
                sync.wait_ge(s_act.h, s_act.n)
                sync.wait_ge(s_pe.h, s_pe.n)
                sync.wait_ge(s_w.h, s_w.n)
                sync.wait_ge(s_dma.h, s_dma.n)
                sync.dma_start(out=y_out.rearrange("(tb p) c -> p tb c", p=128),
                               in_=x_tm[:].rearrange("p (tb c) -> p tb c", c=DM)).then_inc(s_dma.h, 16)
                s_dma.bump(16)
                if dbg_a is not None:
                    sync.dma_start(out=dbg_a, in_=ln_tm[:]).then_inc(s_dma.h, 16)
                    sync.dma_start(out=dbg_b, in_=ln_fm[:]).then_inc(s_dma.h, 16)
                    s_dma.bump(32)
                sync.wait_ge(s_dma.h, s_dma.n)

        @block.sync
        def _(sync):
            for ck in range(4):
                sync.dma_start(
                    out=x_tm[:, ck * 4 * DM:(ck + 1) * 4 * DM].rearrange("p (tb c) -> p tb c", c=DM),
                    in_=x_in.rearrange("(tb p) c -> p tb c", p=128)[:, 4 * ck:4 * ck + 4, :]) \
                    .then_inc(s_xl[ck].h, 16)
                s_xl[ck].bump(16)
            sync.dma_start(out=bias_f, in_=bias_cat).then_inc(s_dma.h, 16)
            s_dma.bump(16)
        p0_loads = s_dma.n

        @block.gpsimd
        def _(g):
            for src, dst in ((g1r.rearrange("c pt p b -> p c pt b"), g1sb),
                             (g2r.rearrange("c p e -> p c e"), g2sb),
                             (g1i.rearrange("c pt p b -> p c pt b"), g1sbi),
                             (g2i.rearrange("c p e -> p c e"), g2sbi)):
                g.dma_start(out=dst, in_=src).then_inc(s_g4.h, 16)
                s_g4.bump(16)
        g4_done = s_g4.n

        @block.gpsimd
        def _(g):
            for src, dst in ((cmask_d, cmask_s[:]), (idn_d, idn_s[:]), (amask_d, amask_s[:]),
                             (gm25_d, gm25_s[:]), (big2_d, big2_s[:]),
                             (tlow_d, tlow_s[:]), (ones2_d, ones2_s[:]), (onescol_d, onescol_s[:]),
                             (epscol_d, epscol_s[:])):
                g.dma_start(out=dst, in_=src).then_inc(s_w.h, 16)
                s_w.bump(16)
        g_loads_done = s_w.n

        def emit_stats(v, t):
            base = t * DM
            s_dve.inc(v.bn_stats(bnst[:, 0:6], x_tm[:, base:base + 512]))
            m2 = s_dve.inc(v.bn_stats(bnst[:, 6:12], x_tm[:, base + 512:base + 1024]))
            v.wait_ge(s_dve.h, m2)
            m3 = s_dve.inc(v.bn_aggr(mv[:, 2 * t:2 * t + 2], bnst[:]))
            v.wait_ge(s_dve.h, m3)
            return m3

        def emit_rstd(sc):
            var_ap = mv[:].rearrange("p (t two) -> p t two", two=2)[:, :, 1:2].squeeze(2)
            m = s_act.inc(sc.activation(lnv[:], var_ap, AF.Ln, bias=epscol_s[:], scale=1.0))
            sc.wait_ge(s_act.h, m)
            return s_act.inc(sc.activation(rstd[:], lnv[:], AF.Exp, scale=-0.5))

        def emit_normalize(v, t):
            base = t * DM
            return s_dve.inc(v.tensor_scalar(ln_tm[:, base:base + DM], x_tm[:, base:base + DM],
                                             mv[:, 2 * t:2 * t + 1], rstd[:, t:t + 1],
                                             AluOpType.subtract, AluOpType.mult))

        @block.vector
        def _(v):
            s_dve.inc(v.memset(glh[:], 0.0))
            v.wait_ge(s_g4.h, g4_done)
            for pt in range(4):
                ops = {}
                for tag, g1t, g2t, acc, scr in (("r", g1sb, g2sb, waccR, wprod),
                                                ("i", g1sbi, g2sbi, waccI, wprod2)):
                    seq = []
                    for c in range(4):
                        def prod(c=c, g1t=g1t, g2t=g2t, acc=acc, scr=scr, pt=pt):
                            a_bc = g1t[:, c, pt, :].unsqueeze(2).broadcast_to((128, 32, 16))
                            b_bc = g2t[:, c, :].unsqueeze(1).broadcast_to((128, 32, 16))
                            dst = acc if c == 0 else scr
                            return v.tensor_tensor(
                                dst.rearrange("p (b e) -> p b e", e=16), a_bc, b_bc,
                                AluOpType.mult)
                        seq.append(prod)
                        if c > 0:
                            seq.append(lambda acc=acc, scr=scr: v.tensor_tensor(
                                acc, acc, scr, AluOpType.add))
                    ops[tag] = seq
                marks = {"r": 0, "i": 0}
                for tag, emit in [z for pair in zip(
                        [("r", o) for o in ops["r"]],
                        [("i", o) for o in ops["i"]]) for z in pair]:
                    if marks[tag]:
                        v.wait_ge(s_dve.h, marks[tag])
                    marks[tag] = s_dve.inc(emit())
                v.wait_ge(s_dve.h, marks["r"])
                s_dve.inc(v.tensor_copy(wffn[:, pt, 0:512], waccR))
                s_dve.inc(v.tensor_copy(wffn[:, pt + 4, 512:1024], waccR))
                v.wait_ge(s_dve.h, marks["i"])
                s_dve.inc(v.tensor_copy(wffn[:, pt, 512:1024], waccI))
                m4 = s_dve.inc(v.tensor_scalar(wffn[:, pt + 4, 0:512], waccI, -1.0, None,
                                               AluOpType.mult))
                v.wait_ge(s_dve.h, m4)
                v.wait_ge(s_xl[pt].h, 16)
                for t in range(4 * pt, 4 * pt + 4):
                    emit_stats(v, t)
        ln1_stats = s_dve.n

        @block.vector
        def _(v):
            v.wait_ge(s_dma.h, p0_loads)
            m16 = s_dve.inc(v.tensor_copy(b16[:], bias_f))
            v.wait_ge(s_dve.h, m16)
            s_dve.inc(v.tensor_tensor(blo[:], bias_f, b16[:], AluOpType.subtract))
            s_dve.inc(v.memset(bdcv[:], 0.0))
            s_dve.inc(v.memset(cumk[:, 0:1], 0.0))
        wffn_done = s_dve.n

        @block.tensor
        def _(te):
            te.wait_ge(s_dve.h, wffn_done)
            te.wait_ge(s_w.h, g_loads_done)
            for hw in range(2):
                te.matmul(PW0[:, hw * 512:hw * 512 + 512], ones2_s[0:1, :],
                          b16[:, hw * 512:hw * 512 + 512], start=True, stop=False)
                mm = te.matmul(PW0[:, hw * 512:hw * 512 + 512], ones2_s[0:1, :],
                               blo[:, hw * 512:hw * 512 + 512], start=False, stop=True)
            s_pe.inc(mm)
        bias_pe = s_pe.n

        @block.vector
        def _(v):
            v.wait_ge(s_pe.h, bias_pe)
            s_dve.inc(v.tensor_copy(bias_sb[:], PW0[:]))
        bias_evac = s_dve.n

        def emit_transposes(wait_dve_at, wait_w_at):
            marks = []

            @block.sync
            def _(sync):
                if not isinstance(wait_dve_at, list):
                    sync.wait_ge(s_dve.h, wait_dve_at)
                if wait_w_at:
                    sync.wait_ge(s_w.h, wait_w_at)
                sync.wait_ge(s_dma.h, s_dma.n)
                for t in range(NT):
                    if isinstance(wait_dve_at, list):
                        sync.wait_ge(s_dve.h, wait_dve_at[t])
                    dst = ln_fm[:].rearrange("p (f tt) -> p f tt", tt=T)[:, :, t * 128:(t + 1) * 128]
                    sync.dma_start_transpose(dst, ln_tm[:, t * DM:(t + 1) * DM]).then_inc(s_tr[t].h, 16)
                    marks.append(s_tr[t].bump(16))
            return marks

        def emit_ln_tail(wait_dve_at, wait_w_at):
            @block.scalar
            def _(sc):
                sc.wait_ge(s_dve.h, wait_dve_at)
                sc.wait_ge(s_w.h, g_loads_done)
                emit_rstd(sc)
            rstd_mark = s_act.n

            norm_marks = []

            @block.vector
            def _(v):
                v.wait_ge(s_act.h, rstd_mark)
                for t in range(NT):
                    norm_marks.append(emit_normalize(v, t))
            return emit_transposes(norm_marks, wait_w_at)

        def emit_mlp(kind, wait_dma_at, with_stats, extra_pe_wait=0, marks_out=None,
                     bias_at=0):
            dve_marks = []
            if marks_out is not None:
                dve_marks = marks_out
            for tb in range(NT):
                @block.tensor
                def _(te, tb=tb):
                    if isinstance(wait_dma_at, list):
                        te.wait_ge(s_tr[tb].h, wait_dma_at[tb])
                    elif tb == 0:
                        te.wait_ge(s_dma.h, wait_dma_at)
                    if tb == 0:
                        if kind == "ffn":
                            te.wait_ge(s_dve.h, wffn_done)
                            te.wait_ge(s_dve.h, bias_evac)
                    if tb == 0 and kind != "ffn":
                        te.wait_ge(s_w.h, extra_pe_wait)
                    if tb >= 2:
                        te.wait_ge(s_dve.h, dve_marks[tb - 2])
                    ps = PW0 if tb % 2 == 0 else PW1
                    wmat = wffn if kind == "ffn" else wout_sb
                    for f in range(NF):
                        lhsT = ln_fm[:, f * T + tb * 128: f * T + tb * 128 + 128]
                        first = f == 0
                        last = f == NF - 1
                        te.matmul(ps[:, 0:512], lhsT, wmat[:, f, 0:512], start=first, stop=last)
                        mm = te.matmul(ps[:, 512:1024], lhsT, wmat[:, f, 512:1024],
                                       start=first, stop=last)
                    s_pe.inc(mm)
                pe_mark = s_pe.n

                @block.vector
                def _(v, tb=tb, pe_mark=pe_mark):
                    v.wait_ge(s_pe.h, pe_mark)
                    ps = PW0 if tb % 2 == 0 else PW1
                    base = tb * DM
                    if kind == "ffn":
                        if bias_at:
                            v.wait_ge(s_dve.h, bias_at)
                        if dve_marks:
                            v.wait_ge(s_dve.h, dve_marks[-1])
                        mt = s_dve.inc(v.tensor_tensor(
                            ffn_tmp[:], ps[:], bias_sb[:], AluOpType.add))
                        v.wait_ge(s_dve.h, mt)
                        m = s_dve.inc(v.scalar_tensor_tensor(
                            x_tm[:, base:base + DM], ffn_tmp[:], 0.5,
                            x_tm[:, base:base + DM], AluOpType.mult, AluOpType.add))
                    else:
                        m = s_dve.inc(v.tensor_tensor(
                            x_tm[:, base:base + DM], ps[:],
                            x_tm[:, base:base + DM], AluOpType.add))
                    dve_marks.append(m)
                    if with_stats:
                        v.wait_ge(s_dve.h, m)
                        emit_stats(v, tb)
            return s_dve.n

        ln1_T = emit_ln_tail(wffn_done, g_loads_done)
        if stop == "ln1":
            _finish()
            return nc

        ffn1_done = emit_mlp("ffn", ln1_T, with_stats=True, bias_at=bias_evac)

        @block.vector
        def _(v):
            v.wait_ge(s_dve.h, ffn1_done)
            s_dve.inc(v.memset(vt[:].rearrange("p (blk c) -> p blk c", c=130)[:, :, 64:65], 1.0))
            s_dve.inc(v.memset(vt[:].rearrange("p (blk c) -> p blk c", c=130)[:, :, 129:130], 1.0))
        if stop == "ffn1":
            _finish()
            return nc

        wq_marks = []

        @block.gpsimd
        def _(g):
            for t_ in range(NT):
                g.wait_ge(s_tr[t_].h, ln1_T[t_])
            for j0 in range(2):
                if j0 > 0:
                    g.wait_ge(s_w.h, wq_marks[-1])
                src_ = w_qkv.rearrange("(fc p) (w c) -> p w fc c", p=128, w=3)[:, :, :, 128 * j0:128 * j0 + 128]
                g.dma_start(out=wqkv_j[:, j0], in_=src_).then_inc(s_w.h, 16)
                wq_marks.append(s_w.bump(16))

        ln2_T = emit_ln_tail(ffn1_done, s_w.n)
        if stop == "ln2":
            _finish()
            return nc

        pe_v_done = []
        prev_j_exp3 = 0
        prev_j_exp2 = 0
        prev_j_evac15 = 0
        last_evac = 0
        wout_loaded = 0

        for j in range(NJ):
            slot = j % 2

            qk_evac_marks = []
            for ii in range(4):
                part = 1 - ii // 2

                @block.tensor
                def _(te, j=j, slot=slot, ii=ii, part=part):
                    if ii == 0:
                        te.wait_ge(s_w.h, wq_marks[j])
                        if j == 0:
                            for t_ in range(8):
                                te.wait_ge(s_tr[t_].h, ln2_T[t_])
                        else:
                            te.wait_ge(s_act.h, prev_j_exp2)
                    if ii == 1:
                        if j == 0:
                            for t_ in range(8, NT):
                                te.wait_ge(s_tr[t_].h, ln2_T[t_])
                        else:
                            te.wait_ge(s_act.h, prev_j_exp3)
                    ps = PSQ if ii % 2 == 0 else PW1
                    if ii >= 2:
                        te.wait_ge(s_act.h, qk_evac_marks[ii - 2])
                    for half in range(2):
                        tq = (ii % 2) * 2 + half
                        for f in range(NF):
                            mm = te.matmul(ps[:, half * 512:half * 512 + 512],
                                           wqkv_j[:, slot, part, f, :],
                                           ln_fm[:, f * T + tq * 512: f * T + (tq + 1) * 512],
                                           start=(f == 0), stop=(f == NF - 1))
                        s_pe.inc(mm)
                pe_mark = s_pe.n

                @block.scalar
                def _(sc, ii=ii, part=part, pe_mark=pe_mark):
                    sc.wait_ge(s_pe.h, pe_mark)
                    ps = PSQ if ii % 2 == 0 else PW1
                    dst = qf if part == 0 else kf
                    s_act.inc(sc.copy(dst[:, (ii % 2) * 1024:(ii % 2) * 1024 + 1024], ps[:]))
                qk_evac_marks.append(s_act.n)

            @block.vector
            def _(v, j=j):
                def step(instr):
                    m = s_dve.inc(instr)
                    v.wait_ge(s_dve.h, m)
                v.wait_ge(s_act.h, qk_evac_marks[1])
                step(v.tensor_reduce(chk[:].unsqueeze(2),
                                     kf[:].rearrange("p (c w) -> p c w", w=64),
                                     axis=AX.X, op=AluOpType.add))
                step(v.tensor_tensor_scan(cumk[:, 1:32], chk[:, 0:31], chk[:, 0:31], 0.0,
                                          AluOpType.add, AluOpType.bypass))
                step(v.tensor_copy(glh[0:64, 0:32], cumk[0:64, :]))
                step(v.tensor_copy(glh[64:128, 32:64], cumk[64:128, :]))
            glh_dve = s_dve.n

            v_evac_marks = []

            def emit_v_phase():
              for tbq in range(4):
                @block.tensor
                def _(te, j=j, slot=slot, tbq=tbq):
                    ps = PB6 if tbq % 2 == 0 else PB7
                    if tbq >= 2:
                        te.wait_ge(s_dve.h, v_evac_marks[tbq - 2])
                    elif j > 0:
                        te.wait_ge(s_dve.h, prev_j_evac15)
                    for tb in range(4 * tbq, 4 * tbq + 4):
                        for f in range(NF):
                            mm = te.matmul(ps[:, (tb % 4) * 128:(tb % 4) * 128 + 128],
                                           ln_fm[:, f * T + tb * 128: f * T + tb * 128 + 128],
                                           wqkv_j[:, slot, 2, f, :],
                                           start=(f == 0), stop=(f == NF - 1))
                        s_pe.inc(mm)
                pe_mark = s_pe.n
                if tbq == 3:
                    pe_v_done.append(pe_mark)

                @block.vector
                def _(v, tbq=tbq, pe_mark=pe_mark):
                    v.wait_ge(s_pe.h, pe_mark)
                    ps = PB6 if tbq % 2 == 0 else PB7
                    dst = vt[:, tbq * 4 * 130:(tbq + 1) * 4 * 130].rearrange(
                        "p (b c) -> p b c", c=130).rearrange(
                        "p b (h c2) -> p b h c2", c2=65)[:, :, :, 0:64]
                    s_dve.inc(v.tensor_copy(
                        dst, ps[:].rearrange("p (b h d) -> p b h d", h=2, d=64)))
                v_evac_marks.append(s_dve.n)

            def emit_aprime():
              nonlocal wout_loaded
              if j + 2 <= NJ - 1 or j == NJ - 1:
                @block.gpsimd
                def _(g, j=j):
                    if j + 2 <= NJ - 1:
                        jl = j + 2
                        g.wait_ge(s_w.h, wq_marks[jl - 1])
                        g.wait_ge(s_pe.h, pe_v_done[j])
                        src_ = w_qkv.rearrange("(fc p) (w c) -> p w fc c", p=128, w=3)[:, :, :, 128 * jl:128 * jl + 128]
                        g.dma_start(out=wqkv_j[:, jl % 2], in_=src_).then_inc(s_w.h, 16)
                        wq_marks.append(s_w.bump(16))
                    if j == NJ - 1:
                        g.wait_ge(s_w.h, wq_marks[-1])
                        g.dma_start(out=wout_sb[:],
                                    in_=w_out.rearrange("(fc p) c -> p fc c", p=128)).then_inc(s_w.h, 16)
                        s_w.bump(16)
              if j == NJ - 1:
                wout_loaded = s_w.n


            def emit_chv():
                @block.tensor
                def _(te):
                    te.wait_ge(s_dve.h, v_evac_marks[3])
                    for p in range(NP):
                        mm = te.matmul(PB6[0:64, 0:130], big2_s[:, 32 - 2 * p:96 - 2 * p],
                                       vt[:, p * 130:(p + 1) * 130],
                                       start=(p == 0), stop=(p == NP - 1))
                    s_pe.inc(mm)
                chv_pe = s_pe.n

                @block.vector
                def _(v, chv_pe=chv_pe):
                    v.wait_ge(s_pe.h, chv_pe)
                    s_dve.inc(v.tensor_copy(chv[:], PB6[0:64, 0:130]))
                return s_dve.n

            def emit_tlow(chv_dve):
                @block.tensor
                def _(te):
                    te.wait_ge(s_dve.h, chv_dve)
                    s_pe.inc(te.matmul(PB6[0:64, 256:386], tlow_s[:], chv[:],
                                       start=True, stop=True))
                return s_pe.n

            def emit_bdcv(cumv_pe):
                @block.vector
                def _(v):
                    def step(instr):
                        m = s_dve.inc(instr)
                        v.wait_ge(s_dve.h, m)
                    v.wait_ge(s_pe.h, cumv_pe)
                    step(v.tensor_copy(bdcv[0:32, 0:64], PB6[0:32, 256:320]))
                    step(v.tensor_copy(bdcv[32:64, 64:128], PB6[32:64, 321:385]))
                return s_dve.n

            sgrp_pe = []
            exp_marks = []
            mask_marks = []
            pv_pe = []
            evac_mult = []
            evac_done = []
            tanh_marks = []
            msig_marks = []
            gate_pe_marks = []

            def emit_sgrp_pe(q, j=j):
                @block.tensor
                def _(te, q=q):
                    if q <= 1:
                        te.wait_ge(s_act.h, qk_evac_marks[2 + q])
                    else:
                        te.wait_ge(s_act.h, exp_marks[q - 2])
                    bank = PSQ if q % 2 == 0 else PW1
                    for pp in range(4):
                        p = 4 * q + pp
                        mmA = te.matmul(bank[:, pp * 128:pp * 128 + 128],
                                        kf[0:64, p * 128:(p + 1) * 128],
                                        qf[0:64, p * 128:(p + 1) * 128], start=True, stop=True)
                        mmB = te.matmul(bank[:, 512 + pp * 128:512 + pp * 128 + 128],
                                        kf[64:128, p * 128:(p + 1) * 128],
                                        qf[64:128, p * 128:(p + 1) * 128], start=True, stop=True)
                        if pp == 3:
                            s_pe.inc(mmA)
                            s_pe.inc(mmB)
                sgrp_pe.append(s_pe.n)

            def emit_sgrp_post(q, j=j):
                @block.scalar
                def _(sc, q=q):
                    sc.wait_ge(s_pe.h, sgrp_pe[q])
                    if q >= 2:
                        sc.wait_ge(s_pe.h, pv_pe[q - 2])
                        sc.wait_ge(s_dve.h, mask_marks[q - 2])
                    bank = PSQ if q % 2 == 0 else PW1
                    s_act.inc(sc.activation(pte2[:, q % 2, :], bank[:], AF.Exp, scale=0.125))
                exp_marks.append(s_act.n)

                @block.vector
                def _(v, q=q):
                    v.wait_ge(s_act.h, exp_marks[q])
                    cm = cmask_s[:].unsqueeze(1).broadcast_to((128, 8, 128))
                    s_dve.inc(v.tensor_tensor(
                        ptm[:, q % 2, :].rearrange("p (a b) -> p a b", b=128),
                        pte2[:, q % 2, :].rearrange("p (a b) -> p a b", b=128),
                        cm, AluOpType.mult))
                mask_marks.append(s_dve.n)

            def emit_gate_pe(sbp, j=j):
                if True:
                    @block.tensor
                    def _(te, sbp=sbp):
                        if sbp == 0:
                            te.wait_ge(s_dve.h, glh_dve)
                        else:
                            te.wait_ge(s_act.h, tanh_marks[sbp - 1])
                        te.wait_ge(s_act.h, qk_evac_marks[2 + sbp])
                        for half in range(2):
                            sbi = sbp * 2 + half
                            mm = te.matmul(PW0[0:64, half * 512:half * 512 + 512],
                                           glh[:, 0:64],
                                           qf[:, sbi * 512:(sbi + 1) * 512], start=True, stop=True)
                            s_pe.inc(mm)
                    gate_pe_marks.append(s_pe.n)

            def emit_gate_post(sbp, j=j):
                if True:
                    pe_mark = gate_pe_marks[sbp]

                    @block.scalar
                    def _(sc, sbp=sbp, pe_mark=pe_mark):
                        sc.wait_ge(s_pe.h, pe_mark)
                        if sbp >= 1:
                            sc.wait_ge(s_dve.h, msig_marks[sbp - 1])
                        s_act.inc(sc.activation(tanh_sb[:], PW0[0:64, :], AF.Tanh, scale=TANH_SCALE))
                    tanh_marks.append(s_act.n)

                    @block.vector
                    def _(v, sbp=sbp):
                        v.wait_ge(s_act.h, tanh_marks[sbp])
                        s_dve.inc(v.scalar_tensor_tensor(
                            msig[:, sbp * 1024:(sbp + 1) * 1024], tanh_sb[:], 1.0,
                            gm25_s[:, sbp * 1024:(sbp + 1) * 1024], AluOpType.add, AluOpType.mult))
                    msig_marks.append(s_dve.n)

            def emit_pv_group(q, bdcv_at=0, j=j):
                @block.tensor
                def _(te, q=q, bdcv_at=bdcv_at):
                    te.wait_ge(s_dve.h, mask_marks[q])
                    if q == 0:
                        te.wait_ge(s_dve.h, msig_marks[0])
                        te.wait_ge(s_act.h, tanh_marks[1])
                        te.wait_ge(s_dve.h, bdcv_at)
                        if j > 0:
                            te.wait_ge(s_dve.h, prev_j_evac15)
                    else:
                        te.wait_ge(s_dve.h, evac_mult[q - 1])
                    if q == 2:
                        te.wait_ge(s_dve.h, msig_marks[1])
                    slotbase = (0, 130, 512, 642)
                    for pp in range(4):
                        p = 4 * q + pp
                        base = slotbase[pp]
                        ptA = ptm[:, q % 2, pp * 128:pp * 128 + 128]
                        ptB = ptm[:, q % 2, 512 + pp * 128:512 + pp * 128 + 128]
                        te.matmul(PW0[:, base:base + 65], ptA,
                                  vt[:, p * 130:p * 130 + 65], start=True, stop=True)
                        mmD = te.matmul(PW0[:, base + 65:base + 130], ptB,
                                        vt[:, p * 130 + 65:p * 130 + 130],
                                        start=True, stop=True)
                    s_pe.inc(mmD)

                    if q >= 1:
                        te.wait_ge(s_dve.h, evac_done[q - 1])
                    for pp in range(4):
                        p = 4 * q + pp
                        cr = PB6 if pp < 2 else PB7
                        mmC = te.matmul(cr[:, (pp % 2) * 128:(pp % 2) * 128 + 128],
                                        msig[:, p * 128:(p + 1) * 128], bdcv[:],
                                        start=True, stop=True)
                    s_pe.inc(mmC)
                pv_pe.append(s_pe.n)

                @block.vector
                def _(v, q=q, j=j):
                    v.wait_ge(s_pe.h, pv_pe[q])
                    marks = []
                    for half in range(2):
                        pvh = PW0[:, half * 512:half * 512 + 260].rearrange(
                            "p (s c2) -> p s c2", c2=65)
                        marks.append(s_dve.inc(v.reciprocal(
                            rc2[:, half, :], pvh[:, :, 64:65].squeeze(2))))
                    for half in range(2):
                        v.wait_ge(s_dve.h, marks[half])
                        pvh = PW0[:, half * 512:half * 512 + 260].rearrange(
                            "p (s c2) -> p s c2", c2=65)
                        marks.append(s_dve.inc(v.tensor_tensor(
                            att2[:, half, :].rearrange("p (s d) -> p s d", d=64),
                            pvh[:, :, 0:64],
                            rc2[:, half, :].unsqueeze(2).broadcast_to((128, 4, 64)),
                            AluOpType.mult)))
                    evac_mult.append(s_dve.n)
                    for half in range(2):
                        v.wait_ge(s_dve.h, marks[2 + half])
                        crb = PB6 if half == 0 else PB7
                        t0 = 4 * q + 2 * half
                        s_dve.inc(v.tensor_tensor(
                            ln_tm[:].rearrange("p (t c) -> p t c", c=DM)[
                                :, t0:t0 + 2, j * 128:j * 128 + 128],
                            crb[:, 0:256].rearrange("p (s c) -> p s c", c=128),
                            att2[:, half, :].rearrange("p (s d) -> p s d", d=128),
                            AluOpType.add))
                evac_done.append(s_dve.n)

            emit_sgrp_pe(0)
            emit_gate_pe(0)
            emit_v_phase()
            emit_aprime()
            emit_sgrp_post(0)
            emit_gate_post(0)
            emit_sgrp_pe(1)
            emit_gate_pe(1)
            chv_dve = emit_chv()
            cumv_pe = emit_tlow(chv_dve)
            bdcv_dve = emit_bdcv(cumv_pe)
            emit_sgrp_post(1)
            emit_gate_post(1)
            emit_pv_group(0, bdcv_dve)
            emit_sgrp_pe(2)
            emit_sgrp_post(2)
            emit_pv_group(1)
            emit_sgrp_pe(3)
            emit_sgrp_post(3)
            emit_pv_group(2)
            emit_pv_group(3)

            prev_j_exp3 = exp_marks[3]
            prev_j_exp2 = exp_marks[2]
            prev_j_evac15 = evac_done[-1]
            last_evac = evac_done[-1]

        if stop == "attn":
            _finish()
            return nc

        @block.tensor
        def _(te):
            te.wait_ge(s_act.h, prev_j_exp3)
            te.wait_ge(s_act.h, s_act.n)
            te.wait_ge(s_dve.h, last_evac)
            for hw in range(2):
                te.matmul(PSQ[:, hw * 512:hw * 512 + 512], ones2_s[0:1, :],
                          b16[:, hw * 512:hw * 512 + 512], start=True, stop=False)
                mm = te.matmul(PSQ[:, hw * 512:hw * 512 + 512], ones2_s[0:1, :],
                               blo[:, hw * 512:hw * 512 + 512], start=False, stop=True)
            s_pe.inc(mm)
        bias2_pe = s_pe.n

        @block.vector
        def _(v):
            v.wait_ge(s_pe.h, bias2_pe)
            s_dve.inc(v.tensor_copy(bias_sb[:], PSQ[:]))
        bias_evac2 = s_dve.n

        attn_T = emit_transposes(last_evac, wout_loaded)

        wout_done = emit_mlp("wout", attn_T, with_stats=True, extra_pe_wait=wout_loaded)
        if stop == "wout":
            _finish()
            return nc

        ln3_T = emit_ln_tail(wout_done, s_w.n)

        ffn2_marks = []
        ffn2_done = emit_mlp("ffn", ln3_T, with_stats=False, marks_out=ffn2_marks,
                             bias_at=bias_evac2)

        @block.sync
        def _(sync):
            for tb in range(NT):
                sync.wait_ge(s_dve.h, ffn2_marks[tb])
                sync.dma_start(out=y_out.rearrange("(tb p) c -> p tb c", p=128)[:, tb:tb + 1, :],
                               in_=x_tm[:, tb * DM:(tb + 1) * DM].unsqueeze(1)).then_inc(s_dma.h, 16)
                s_dma.bump(16)
            sync.wait_ge(s_dma.h, s_dma.n)

    return nc



def _consts():
    cmask = np.zeros((128, 128), np.float32)
    s2g, s1g = np.meshgrid(np.arange(128), np.arange(128), indexing="ij")
    cmask[(s2g // 64 == s1g // 64) & (s2g % 64 <= s1g % 64)] = 1.0
    gm25 = np.zeros((64, T), np.float32)
    ch = (np.arange(T) // 64)
    for r in range(64):
        gm25[r, :] = np.where((r % 32) == ch, 0.25, 0.0)
    big2 = np.zeros((128, 160), np.float32)
    tvec = np.arange(128)
    big2[tvec < 64, 32] = 1.0
    big2[tvec >= 64, 33] = 1.0
    big2[tvec < 64, 64] = 1.0
    big2[tvec >= 64, 65] = 1.0
    tlow = np.zeros((64, 64), np.float32)
    for c in range(32):
        for cp in range(32):
            if c < cp:
                tlow[c, cp] = 1.0 / 64.0
                tlow[32 + c, 32 + cp] = 1.0 / 64.0
    ones2 = np.ones((2, 128), np.float32)
    onescol = np.ones((128, 1), np.float32)
    epscol = np.full((128, 1), EPS, np.float32)
    idn = np.eye(128, dtype=np.float32)
    amask = np.where(np.tile(cmask, (1, 4)) > 0, 0.0, -16384.0).astype(np.float32)
    return cmask, gm25, big2, tlow, ones2, onescol, epscol, idn, amask


def _expand_g1(G1):
    out = np.empty((4, 4, 128, 32), np.float32)
    for c in range(4):
        for pt in range(4):
            d = 8 * pt + (np.arange(128) // 16)
            out[c, pt] = G1[:, c, :][:, d].T
    return out


def _expand_g2(G2):
    out = np.empty((4, 128, 16), np.float32)
    g = np.arange(128) % 16
    for c in range(4):
        out[c] = G2[c, :, :].T[g, :]
    return out


def kernel(L0_attn_x, L0_attn_W_qkv, L0_attn_W_out, L0_ffn_G1_R, L0_ffn_G2_R,
           L0_ffn_G1_I, L0_ffn_G2_I, L0_ffn_bias_real, L0_ffn_bias_imag):
    B = L0_attn_x.shape[0]
    assert B == 8 and L0_attn_x.shape[1] == T and L0_attn_x.shape[2] == DM

    G1R = np.asarray(L0_ffn_G1_R, np.float32)[0]
    G1I = np.asarray(L0_ffn_G1_I, np.float32)[0]
    G2R = np.asarray(L0_ffn_G2_R, np.float32)[:, :, 0, :]
    G2I = np.asarray(L0_ffn_G2_I, np.float32)[:, :, 0, :]
    g1r_e = _expand_g1(G1R)
    g1i_e = _expand_g1(G1I)
    g2r_e = _expand_g2(G2R)
    g2i_e = _expand_g2(G2I)
    bias_cat = np.concatenate([np.asarray(L0_ffn_bias_real, np.float32),
                               np.asarray(L0_ffn_bias_imag, np.float32)])[None, :]
    cmask, gm25, big2, tlow, ones2, onescol, epscol, idn, amask = _consts()

    wqkv = np.ascontiguousarray(np.asarray(L0_attn_W_qkv, np.float32))
    wout = np.ascontiguousarray(np.asarray(L0_attn_W_out, np.float32))
    x = np.ascontiguousarray(np.asarray(L0_attn_x, np.float32))

    nc = build_program()
    in_maps = []
    for b in range(B):
        in_maps.append(dict(
            x_in=x[b], w_qkv=wqkv, w_out=wout,
            g1r=g1r_e, g2r=g2r_e, g1i=g1i_e, g2i=g2i_e,
            bias_cat=bias_cat, cmask=cmask, gm25=gm25, big2=big2,
            tlow=tlow, ones2=ones2, onescol=onescol, epscol=epscol,
            idn=idn, amask=amask,
        ))
    res = run_bass_kernel_spmd(nc, in_maps, core_ids=list(range(B)))
    out = np.stack([np.asarray(res.results[b]["y_out"]) for b in range(B)], axis=0)
    return out.astype(np.float32)
